# version 12
# baseline (speedup 1.0000x reference)
import numpy as np
from contextlib import ExitStack

import concourse.bass as bass
import concourse.mybir as mybir
from concourse.bass_utils import run_bass_kernel_spmd

AF = mybir.ActivationFunctionType
OP = mybir.AluOpType
F32 = mybir.dt.float32
BF16 = mybir.dt.bfloat16

NCORES = 8
BATCH = 2
L = 4096
D = 2048
P = 128
C = D // P
TLOC = L * BATCH // NCORES
TH = 512
NEC = 4
SCALE = float(P) ** -0.5
EPS = 1e-5

CN_M33, CN_P33, CN_M02, CN_P02, CN_EPS, CN_MH = 0, 1, 2, 3, 4, 5
ENGINES = ("sync", "gpsimd", "vector", "scalar", "tensor")


class Sched:

    def __init__(self):
        self.ops = []
        self.last_write = {}
        self.readers = {}
        self.bar = set()
        self.bar_eng = ()

    def barrier(self, engines=ENGINES, only=()):
        self.bar_eng = only
        last = {}
        for op in self.ops:
            if op["eng"] in engines:
                last[op["eng"]] = op["i"]
        self.bar = set(last.values())

    def add(self, eng, fn, reads=(), writes=(), dma=False, cc=False):
        deps = set(self.bar) if (not self.bar_eng or eng in self.bar_eng) else set()
        for t in reads:
            w = self.last_write.get(t)
            if w is not None:
                deps.add(w)
        for t in writes:
            w = self.last_write.get(t)
            if w is not None:
                deps.add(w)
            for r in self.readers.get(t, ()):
                deps.add(r)
        i = len(self.ops)
        self.ops.append(dict(i=i, eng=eng, fn=fn, dma=dma, cc=cc,
                             deps=deps, need=False, val=None, waits=None))
        for t in reads:
            self.readers.setdefault(t, []).append(i)
        for t in writes:
            self.last_write[t] = i
            self.readers[t] = []
        return i

    def resolve(self):
        ops = self.ops

        def cat(op):
            return "x" if op["cc"] else ("d" if op["dma"] else "c")

        for op in ops:
            best = {}
            for d in op["deps"]:
                k = (ops[d]["eng"], cat(ops[d]))
                if d > best.get(k, -1):
                    best[k] = d
            op["deps"] = set(best.values())
        for op in ops:
            if op["dma"] or op["cc"]:
                op["need"] = True
        for op in ops:
            for d in op["deps"]:
                dep = ops[d]
                if dep["eng"] == op["eng"] and cat(dep) == "c":
                    continue
                dep["need"] = True
        keys = [(e, c) for e in ENGINES for c in ("c", "d")] + [("gpsimd", "x")]
        cum = {k: 0 for k in keys}
        for op in ops:
            k = (op["eng"], cat(op))
            if op["need"]:
                cum[k] += 16 if op["dma"] else 1
            op["val"] = cum[k]
            op["key"] = k
        vc = {e: {k: 0 for k in keys} for e in ENGINES}
        snap = {}
        for op in ops:
            E = op["eng"]
            waits = {}
            for d in sorted(op["deps"]):
                dep = ops[d]
                if dep["eng"] == E and cat(dep) == "c":
                    continue
                k, v = dep["key"], dep["val"]
                if vc[E][k] >= v:
                    continue
                waits[k] = max(waits.get(k, 0), v)
            for k, v in waits.items():
                if v > vc[E][k]:
                    vc[E][k] = v
                s = snap.get((k, v))
                if s:
                    for k3, v3 in s.items():
                        if v3 > vc[E][k3]:
                            vc[E][k3] = v3
            op["waits"] = waits
            if op["need"]:
                s = dict(vc[E])
                s[op["key"]] = op["val"]
                snap[(op["key"], op["val"])] = s
        return cum


def build_nc(max_ops=None, info=None, debug=False, barriers=(), cast_eng="gpsimd"):
    nc = bass.Bass("TRN2", target_bir_lowering=False)
    if debug:
        dbg_d = {n: nc.dram_tensor(f"dbg_{n}", [P, C * TLOC], BF16, kind="ExternalOutput")
                 for n in ("kv", "phiq", "xqf", "ctx", "xn")}
        dbg_gl = nc.dram_tensor("dbg_gl", [P, C], F32, kind="ExternalOutput")
        dbg_u = nc.dram_tensor("dbg_u", [P, TLOC], BF16, kind="ExternalOutput")
        dbg_pk = nc.dram_tensor("dbg_pk", [P, TLOC], BF16, kind="ExternalOutput")
        dbg_S = nc.dram_tensor("dbg_S", [P, NCORES * C], F32, kind="ExternalOutput")
        dbg_ctb = nc.dram_tensor("dbg_ctb", [P, C], F32, kind="ExternalOutput")

    xt_d = nc.dram_tensor("xt", [TLOC, D], F32, kind="ExternalInput")
    xf_d = nc.dram_tensor("xf", [D, TLOC], F32, kind="ExternalInput")
    wq4_d = nc.dram_tensor("wq4", [C, P, C, P], F32, kind="ExternalInput")
    wk4_d = nc.dram_tensor("wk4", [C, P, C, P], F32, kind="ExternalInput")
    wv4_d = nc.dram_tensor("wv4", [C, P, C, P], F32, kind="ExternalInput")
    wo4_d = nc.dram_tensor("wo4", [NEC, P, C, TH], F32, kind="ExternalInput")
    wf4_d = nc.dram_tensor("wf4", [NEC, P, C, TH], F32, kind="ExternalInput")
    rr4_d = nc.dram_tensor("rr4", [NEC, P, C, TH], F32, kind="ExternalInput")
    bq_d = nc.dram_tensor("bq", [P, C], F32, kind="ExternalInput")
    bk_d = nc.dram_tensor("bk", [P, C], F32, kind="ExternalInput")
    bv_d = nc.dram_tensor("bv", [P, C], F32, kind="ExternalInput")
    bo_d = nc.dram_tensor("bo", [1, D], F32, kind="ExternalInput")
    maskb_d = nc.dram_tensor("maskb", [P, NCORES * C], F32, kind="ExternalInput")
    out_d = nc.dram_tensor("out", [TLOC, D], F32, kind="ExternalOutput")

    in_cc = nc.dram_tensor("in_cc", [1, D], F32)
    out_cc = nc.dram_tensor("out_cc", [NCORES, D], F32)

    es = ExitStack()
    sb = lambda n, s, d: es.enter_context(nc.sbuf_tensor(n, s, d))
    pt = lambda n, s: es.enter_context(nc.psum_tensor(n, s, F32))

    with es:
        xnf = sb("xnf", [P, C * TLOC], BF16)
        xqf = sb("xqf", [P, C * TLOC], BF16)
        phiq = sb("phiq", [P, C * TLOC], BF16)
        kv = sb("kv", [P, C * TLOC], BF16)

        wst = sb("wst", [P, C * P], F32)
        wkb = [sb("wkb0", [P, C * P], BF16), sb("wkb1", [P, C * P], BF16)]
        wvb = [sb("wvb0", [P, C * P], BF16), sb("wvb1", [P, C * P], BF16)]
        gst_r = [sb("gstr0", [P, TH], F32), sb("gstr1", [P, TH], F32)]

        x_st = [sb(f"x_st{i}", [P, TH], F32) for i in range(4)]
        x2b = [sb(f"x2b{i}", [P, TH], BF16) for i in range(2)]
        xn_a = sb("xn_a", [P, TH], F32)
        xn_b = [sb(f"xn_b{i}", [P, TH], F32) for i in range(2)]
        s1_t = [sb(f"s1t{i}", [P, TH], BF16) for i in range(2)]
        s2_t = [sb(f"s2t{i}", [P, TH], BF16) for i in range(2)]
        tq_t = sb("tq_t", [P, TH], BF16)
        e_t = [sb(f"et{i}", [P, TH], F32) for i in range(2)]
        a_t = sb("a_t", [P, TH], F32)
        phik = sb("phik", [P, TLOC], BF16)
        u_t = sb("u_t", [P, TLOC], BF16)
        ones_bf = sb("ones_bf", [P, TLOC], BF16)

        rows = sb("rows", [1, 2560], F32)
        bo_b = [sb(f"bo_b{i}", [1, TH], BF16) for i in range(2)]

        bq_s = sb("bq_s", [P, C], F32)
        bk_s = sb("bk_s", [P, C], F32)
        bv_s = sb("bv_s", [P, C], F32)
        bvS = sb("bvS", [P, C], F32)
        maskb_s = sb("maskb_s", [P, NCORES * C], F32)
        S_s = sb("S_s", [P, NCORES * C], F32)
        masked_s = sb("masked_s", [P, NCORES * C], F32)
        gl_s = sb("gl_s", [P, C], F32)
        ctb = sb("ctb", [P, C], F32)

        ones_col = sb("ones_col", [P, 1], F32)
        ones_colb = sb("ones_colb", [P, 1], BF16)
        ones_rowf = sb("ones_rowf", [1, P], F32)
        ones_rowb = sb("ones_rowb", [1, P], BF16)
        consts = sb("consts", [P, 8], F32)
        fl_scr = sb("fl_scr", [1, 4], F32)

        psn = ["psa", "psb", "psc", "psd"]
        pst = {n: pt(n, [P, 2 * TH]) for n in psn}
        half = lambda n, h: pst[n][:, h * TH:(h + 1) * TH]
        ptok = lambda n, h: f"{n}{h}"
        psG = [(n, h) for n in psn for h in range(2)]

        S = Sched()

        def dve_flush(src_ap, tokens):
            S.add("vector", lambda e, a=src_ap: e.tensor_copy(fl_scr[0:1, 0:1], a),
                  reads=list(tokens), writes=list(tokens))

        def dma(eng, out, in_, reads, writes):
            S.add(eng, lambda e, o=out, i=in_: e.dma_start(out=o, in_=i),
                  reads=reads, writes=writes, dma=True)

        S.add("gpsimd", lambda e: e.memset(ones_col[:], 1.0), writes=["onc"])
        S.add("gpsimd", lambda e: e.memset(ones_colb[:], 1.0), writes=["oncb"])
        S.add("gpsimd", lambda e: e.memset(ones_rowf[:], 1.0), writes=["onrf"])
        S.add("gpsimd", lambda e: e.memset(ones_rowb[:], 1.0), writes=["onrb"])
        S.add("gpsimd", lambda e: e.memset(ones_bf[:], 1.0), writes=["onbf"])
        for col, val in [(CN_M33, -0.33), (CN_P33, 0.33), (CN_M02, -0.2),
                         (CN_P02, 0.2), (CN_EPS, EPS), (CN_MH, -0.5)]:
            S.add("gpsimd", lambda e, c=col, v=val: e.memset(consts[:, c:c + 1], v),
                  writes=[f"cn{col}"])
        dma("sync", bq_s[:], bq_d[:, :], [], ["bq"])
        dma("sync", bk_s[:], bk_d[:, :], [], ["bk"])
        dma("sync", bv_s[:], bv_d[:, :], [], ["bv"])
        dma("sync", maskb_s[:], maskb_d[:, :], [], ["maskb"])
        S.add("vector", lambda e: e.tensor_scalar(bvS[:], bv_s[:], SCALE, None, OP.mult),
              reads=["bv"], writes=["bvS"])

        def load_qkv_weight(wsrc, dstb, dsttok, scale=None, eng=None):
            dma("sync", wst[:], wsrc, ["wst.0", "wst.1", "wst.2", "wst.3"],
                ["wst.0", "wst.1", "wst.2", "wst.3"])
            if scale is None:
                e_ = eng or cast_eng
                S.add(e_, lambda e, d=dstb: e.tensor_copy(d[:], wst[:]),
                      reads=["wst.0", "wst.1", "wst.2", "wst.3"], writes=[dsttok])
                if e_ == "vector":
                    dve_flush(dstb[0:1, 0:1], [dsttok])
            else:
                S.add("vector", lambda e, d=dstb, sc=scale: e.tensor_scalar(d[:], wst[:], sc, None, OP.mult),
                      reads=["wst.0", "wst.1", "wst.2", "wst.3"], writes=[dsttok])
                dve_flush(dstb[0:1, 0:1], [dsttok])

        def load_b(c):
            p = c % 2
            load_qkv_weight(wk4_d[c], wkb[p], f"wkb{p}.o", None)
            load_qkv_weight(wv4_d[c], wvb[p], f"wvb{p}", SCALE)

        load_b(0)
        load_b(1)

        mrow = rows[0:1, 0:TH]
        vrow = rows[0:1, TH:2 * TH]
        trow = rows[0:1, 2 * TH:3 * TH]
        lrow = rows[0:1, 3 * TH:4 * TH]
        rrow = rows[0:1, 4 * TH:5 * TH]
        for h in range(2):
            t0 = h * TH
            pss = psn[2 * h]
            for dc in range(C):
                xr = x_st[dc % 2]
                xb = x2b[dc % 2]
                xtok = f"xs{dc % 2}"
                dma("gpsimd", xr[:], xf_d[dc * P:(dc + 1) * P, t0:t0 + TH], [xtok], [xtok])
                S.add("scalar", lambda e, a=xb, b=xr: e.activation(a[:], b[:], AF.Square),
                      reads=[xtok], writes=[f"x2b{dc % 2}"])
                S.add("tensor", lambda e, b=xr, s=(dc == 0), p=(dc == C - 1), ps=pss:
                      e.matmul(half(ps, 0)[0:1, :], ones_col[:], b[:], start=s, stop=p,
                               skip_group_check=True),
                      reads=[xtok, "onc"], writes=[ptok(pss, 0)])
                S.add("tensor", lambda e, b=xb, s=(dc == 0), p=(dc == C - 1), ps=pss:
                      e.matmul(half(ps, 1)[0:1, :], ones_colb[:], b[:], start=s, stop=p,
                               skip_group_check=True),
                      reads=[f"x2b{dc % 2}", "oncb"], writes=[ptok(pss, 1)])
        for h in range(2):
            pss = psn[2 * h]
            S.add("vector", lambda e, ps=pss: e.tensor_scalar(mrow, half(ps, 0)[0:1, :], 1.0 / D, None, OP.mult),
                  reads=[ptok(pss, 0)], writes=["rows"])
            S.add("vector", lambda e, ps=pss: e.tensor_scalar(vrow, half(ps, 1)[0:1, :], 1.0 / D, None, OP.mult),
                  reads=[ptok(pss, 1)], writes=["rows"])
            S.add("scalar", lambda e: e.activation(trow, mrow, AF.Square),
                  reads=["rows"], writes=["rows"])
            S.add("vector", lambda e: e.tensor_tensor(lrow, vrow, trow, OP.subtract),
                  reads=["rows"], writes=["rows"])
            S.add("scalar", lambda e: e.activation(trow, lrow, AF.Ln,
                                                   bias=consts[0:1, CN_EPS:CN_EPS + 1], scale=1.0),
                  reads=["rows", f"cn{CN_EPS}"], writes=["rows"])
            S.add("scalar", lambda e: e.activation(rrow, trow, AF.Exp, bias=0.0,
                                                   scale=consts[0:1, CN_MH:CN_MH + 1]),
                  reads=["rows", f"cn{CN_MH}"], writes=["rows"])
            S.add("tensor", lambda e, hh=h: e.matmul(half("psb", hh), ones_rowf[:], mrow,
                                                     start=True, stop=True, skip_group_check=True),
                  reads=["rows", "onrf"], writes=[ptok("psb", h)])
            S.add("tensor", lambda e, hh=h: e.matmul(half("psd", hh), ones_rowf[:], rrow,
                                                     start=True, stop=True, skip_group_check=True),
                  reads=["rows", "onrf"], writes=[ptok("psd", h)])
        for dc in range(C):
            for h in range(2):
                t0 = h * TH
                xr = x_st[2 + h]
                xtok = f"xs{2 + h}"
                nb = xn_b[h]
                nbt = f"xnb{h}"
                sl = slice(dc * TLOC + t0, dc * TLOC + t0 + TH)
                dma("gpsimd", xr[:], xf_d[dc * P:(dc + 1) * P, t0:t0 + TH], [xtok], [xtok])
                S.add("vector", lambda e, b=xr, hh=h: e.tensor_tensor(xn_a[:], b[:], half("psb", hh), OP.subtract),
                      reads=[xtok, ptok("psb", h)], writes=["xna"])
                S.add("vector", lambda e, n=nb, hh=h: e.tensor_tensor(n[:], xn_a[:], half("psd", hh), OP.mult),
                      reads=["xna", ptok("psd", h)], writes=[nbt])
                S.add("vector", lambda e, n=nb, s=sl: e.tensor_copy(xnf[:, s], n[:]),
                      reads=[nbt], writes=[f"xnf.{dc}.{h}"])
                dve_flush(xnf[0:1, sl.start:sl.start + 1], [f"xnf.{dc}.{h}"])
                S.add("scalar", lambda e, n=nb, t=s1_t[h]: e.activation(
                    t[:], n[:], AF.Sign, bias=consts[:, CN_M33:CN_M33 + 1], scale=1.0),
                    reads=[nbt, f"cn{CN_M33}"], writes=[f"s1{h}"])
                S.add("scalar", lambda e, n=nb, t=s2_t[h]: e.activation(
                    t[:], n[:], AF.Sign, bias=consts[:, CN_P33:CN_P33 + 1], scale=1.0),
                    reads=[nbt, f"cn{CN_P33}"], writes=[f"s2{h}"])
                S.add("vector", lambda e, u=s1_t[h], v=s2_t[h]: e.tensor_tensor(
                    tq_t[:], u[:], v[:], OP.add),
                    reads=[f"s1{h}", f"s2{h}"], writes=["tq"])
                S.add("vector", lambda e, s=sl: e.tensor_scalar(xqf[:, s], tq_t[:], 0.25, None, OP.mult),
                      reads=["tq"], writes=[f"xqf.{dc}.{h}"])
                dve_flush(xqf[0:1, sl.start:sl.start + 1], [f"xqf.{dc}.{h}"])

        xnf_all = [f"xnf.{dc}.{h}" for dc in range(C) for h in range(2)]
        for b in barriers:
            if b == "AB":
                S.barrier()
            elif b.startswith("AB:"):
                spec = b.split(":")[1]
                if "@" in spec:
                    src, dst = spec.split("@")
                    S.barrier(tuple(src.split("+")), only=tuple(dst.split("+")))
                else:
                    S.barrier(tuple(spec.split("+")))

        def qkv_mm(psname, wb, wtok, c, dc, h):
            xsl = slice(dc * TLOC + h * TH, dc * TLOC + (h + 1) * TH)
            rds = [wtok, f"xnf.{dc}.{h}"]
            if c == 0:
                rds.append(f"xqf.{dc}.{h}")
            S.add("tensor", lambda e, w=wb, d=dc, x=xsl, ps=psname, hh=h,
                  s=(dc == 0), p=(dc == C - 1):
                  e.matmul(half(ps, hh), w[:, d * P:(d + 1) * P], xnf[:, x],
                           start=s, stop=p, skip_group_check=True),
                  reads=rds, writes=[ptok(psname, h)])

        def phi_block(psname, bias_s, btok, c, outbuf, base, outtok_fn):
            for h in range(2):
                et = e_t[h]
                S.add("scalar", lambda e, t=et, ps=psname, hh=h, cc=c, b=bias_s:
                      e.activation(t[:], half(ps, hh), AF.Exp, bias=b[:, cc:cc + 1], scale=1.0),
                      reads=[ptok(psname, h), btok], writes=[f"et{h}"])
                S.add("vector", lambda e, ps=psname, hh=h, cc=c, b=bias_s:
                      e.tensor_scalar(a_t[:], half(ps, hh), b[:, cc:cc + 1], 0.0, OP.add, OP.max),
                      reads=[ptok(psname, h), btok, f"et{h}"], writes=["at"])
                S.add("vector", lambda e, t=et, o=outbuf, sl=slice(base + h * TH, base + (h + 1) * TH):
                      e.scalar_tensor_tensor(o[:, sl], t[:], 1.0, a_t[:], OP.min, OP.add),
                      reads=[f"et{h}", "at"], writes=[outtok_fn(h)])

        for c in range(C):
            p = c % 2
            kps = psn[2 * (c % 2)]
            vps = psn[2 * (c % 2) + 1]
            if 1 <= c < C - 1:
                load_b(c + 1)
            for dc in range(C):
                for h in range(2):
                    qkv_mm(kps, wkb[p], f"wkb{p}.o", c, dc, h)
                    qkv_mm(vps, wvb[p], f"wvb{p}", c, dc, h)
            phi_block(kps, bk_s, "bk", c, phik, 0, lambda h: f"phik{h}")
            for h in range(2):
                S.add("vector", lambda e, ps=vps, hh=h, cc=c,
                      sl=slice(h * TH, (h + 1) * TH):
                      e.scalar_tensor_tensor(u_t[:, sl], half(ps, hh), bvS[:, cc:cc + 1],
                                             phik[:, sl], OP.add, OP.mult),
                      reads=[ptok(vps, h), "bvS", f"phik{h}"], writes=[f"u{h}"])
            S.add("vector", lambda e, cc=c: e.tensor_tensor_scan(
                kv[:, cc * TLOC:(cc + 1) * TLOC], u_t[:], ones_bf[:], 0.0, OP.add, OP.mult),
                reads=["u0", "u1", "onbf"], writes=[f"kv.{c}"])

        def emit_collective():
            if "CX" in barriers:
                S.barrier()
            kvv = kv.ap().rearrange("p (c t) -> p c t", c=C, t=TLOC)
            ctbv = ctb.ap().rearrange("p (c o) -> p c o", c=C, o=1)
            kv_all = [f"kv.{c}" for c in range(C)]
            S.add("vector", lambda e: e.tensor_copy(ctbv, kvv[:, :, TLOC - 1:TLOC]),
                  reads=kv_all, writes=["ctb"])
            dve_flush(ctb[0:1, 0:1], ["ctb"])
            S.add("gpsimd", lambda e: e.dma_start(
                out=in_cc.ap().rearrange("r (p c) -> p r c", p=P, c=C),
                in_=ctb.ap().rearrange("p (r c) -> p r c", r=1, c=C)),
                reads=["ctb"], writes=["in_cc"], dma=True)
            S.add("gpsimd", lambda e: e.collective_compute(
                "AllGather", OP.bypass, replica_groups=[list(range(NCORES))],
                ins=[in_cc[:, :]], outs=[out_cc[:, :]]),
                reads=["in_cc"], writes=["out_cc"], cc=True)
            S.add("gpsimd", lambda e: e.dma_start(
                out=S_s.ap().rearrange("p (r c) -> p r c", r=NCORES, c=C),
                in_=out_cc.ap().rearrange("r (p c) -> p r c", p=P, c=C)),
                reads=["out_cc"], writes=["S"], dma=True)

            if debug:
                all_xnf0 = [f"xnf.{c}.{h}" for c in range(C) for h in range(2)]
                dma("gpsimd", dbg_d["xn"][:, :], xnf[:], all_xnf0, ["dbgxn"])
                dma("gpsimd", dbg_u[:, :], u_t[:], ["u0", "u1"], ["dbgu"])
                dma("gpsimd", dbg_pk[:, :], phik[:], ["phik0", "phik1"], ["dbgpk"])

        if "BC" in barriers:
            S.barrier()
        def emit_collective_finish():
            S.add("vector", lambda e: e.tensor_mul(masked_s[:], S_s[:], maskb_s[:]),
                  reads=["S", "maskb"], writes=["masked"])
            S.add("vector", lambda e: e.reduce_sum(
                gl_s[:], masked_s.ap().rearrange("p (r c) -> p c r", r=NCORES, c=C),
                axis=mybir.AxisListType.X),
                reads=["masked"], writes=["gl"])

        def load_cq(c):
            load_qkv_weight(wq4_d[c], wkb[c % 2], f"wkb{c % 2}.o", None, eng="vector")

        load_cq(0)
        for c in range(C):
            p = c % 2
            qps = psn[2 * (c % 2)]
            if c + 1 < C:
                load_cq(c + 1)
            for dc in range(C):
                for h in range(2):
                    qkv_mm(qps, wkb[p], f"wkb{p}.o", c, dc, h)
            phi_block(qps, bq_s, "bq", c, phiq, c * TLOC, lambda h, cc=c: f"phiq.{cc}.{h}")
            if c == 2:
                emit_collective()
        emit_collective_finish()

        ctxf = xnf
        for c in range(C):
            sl = slice(c * TLOC, (c + 1) * TLOC)
            S.add("vector", lambda e, s=sl, cc=c: e.scalar_tensor_tensor(
                ctxf[:, s], kv[:, s], gl_s[:, cc:cc + 1], phiq[:, s], OP.add, OP.mult),
                reads=[f"kv.{c}", "gl", f"phiq.{c}.0", f"phiq.{c}.1"],
                writes=[f"xnf.{c}.0", f"xnf.{c}.1"])
            dve_flush(ctxf[0:1, sl.start:sl.start + 1], [f"xnf.{c}.0", f"xnf.{c}.1"])

        if "XG" in barriers:
            S.barrier()
        dma("sync", rows[0:1, 0:4 * TH], bo_d[:, :], ["rows"], ["rows"])
        wob = [wkb[0][:, 0:TH], wkb[1][:, 0:TH]]
        wfb = [wkb[0][:, TH:2 * TH], wkb[1][:, TH:2 * TH]]
        gst_o = [wst[:, 0:TH], wst[:, TH:2 * TH]]
        gst_f = [wst[:, 2 * TH:3 * TH], wst[:, 3 * TH:4 * TH]]
        xtb = [x_st[0], x_st[1], xn_b[0], xn_b[1]]
        xttok = ["xs0", "xs1", "xnb0", "xnb1"]

        def load_g(ec, dc):
            p = dc % 2
            dma("sync", gst_o[p], wo4_d[ec][:, dc, :], [f"wst.{p}"], [f"wst.{p}"])
            S.add(cast_eng, lambda e, w=wob[p], g=gst_o[p]: e.tensor_copy(w[:], g[:]),
                  reads=[f"wst.{p}"], writes=[f"wkb{p}.o"])
            dma("sync", gst_f[p], wf4_d[ec][:, dc, :], [f"wst.{2 + p}"], [f"wst.{2 + p}"])
            dma("sync", gst_r[p][:], rr4_d[ec][:, dc, :], [f"gstr{p}"], [f"gstr{p}"])
            S.add("scalar", lambda e, g=gst_f[p], t=s1_t[p]: e.activation(
                t[:], g[:], AF.Sign, bias=consts[:, CN_M02:CN_M02 + 1], scale=1.0),
                reads=[f"wst.{2 + p}", f"cn{CN_M02}"], writes=[f"s1{p}"])
            S.add("scalar", lambda e, g=gst_f[p], t=s2_t[p]: e.activation(
                t[:], g[:], AF.Sign, bias=consts[:, CN_P02:CN_P02 + 1], scale=1.0),
                reads=[f"wst.{2 + p}", f"cn{CN_P02}"], writes=[f"s2{p}"])
            S.add("vector", lambda e, u=s1_t[p], v=s2_t[p]: e.tensor_tensor(
                tq_t[:], u[:], v[:], OP.add),
                reads=[f"s1{p}", f"s2{p}"], writes=["tq"])
            S.add("vector", lambda e, w=wfb[p], g=gst_r[p]: e.scalar_tensor_tensor(
                w[:], tq_t[:], 0.3, g[:], OP.mult, OP.add),
                reads=["tq", f"gstr{p}"], writes=[f"wkb{p}.f", f"wkb{p}.o"])
            dve_flush(wfb[p][0:1, 0:1], [f"wkb{p}.f"])

        load_g(0, 0)
        for ec in range(NEC):
            bop = ec % 2
            S.add("gpsimd", lambda e, c=ec, b=bo_b[bop]: e.tensor_copy(
                b[:], rows[0:1, c * TH:(c + 1) * TH]),
                reads=["rows"], writes=[f"bob{bop}"])
            for dc in range(C):
                p = dc % 2
                if dc + 1 < C:
                    load_g(ec, dc + 1)
                elif ec + 1 < NEC:
                    load_g(ec + 1, 0)
                if 12 <= dc:
                    tt = dc - 12
                    dma("sync", xtb[tt][:], xt_d[tt * P:(tt + 1) * P, ec * TH:(ec + 1) * TH],
                        [xttok[tt]], [xttok[tt]])
                for tt in range(8):
                    hh = tt // 4
                    csl = slice(dc * TLOC + tt * P, dc * TLOC + (tt + 1) * P)
                    gn, gh = psG[tt]
                    S.add("tensor", lambda e, x=csl, w=wob[p], n=gn, g=gh, s=(dc == 0):
                          e.matmul(half(n, g), ctxf[:, x], w[:], start=s, stop=False,
                                   skip_group_check=True),
                          reads=[f"xnf.{dc}.{hh}", f"wkb{p}.o"], writes=[ptok(gn, gh)])
                    S.add("tensor", lambda e, x=csl, w=wfb[p], n=gn, g=gh:
                          e.matmul(half(n, g), xqf[:, x], w[:], start=False, stop=False,
                                   skip_group_check=True),
                          reads=[f"xqf.{dc}.{hh}", f"wkb{p}.f"], writes=[ptok(gn, gh)])
            for tt in range(8):
                gn, gh = psG[tt]
                xi = tt % 4
                op_ = 2 + tt % 2
                S.add("tensor", lambda e, n=gn, g=gh, b=bo_b[bop]:
                      e.matmul(half(n, g), ones_rowb[:], b[:], start=False, stop=True,
                               skip_group_check=True),
                      reads=[f"bob{bop}", "onrb"], writes=[ptok(gn, gh)])
                S.add("vector", lambda e, n=gn, g=gh, a=xtb[xi], b=x_st[op_]:
                      e.tensor_tensor(b[:], a[:], half(n, g), OP.add),
                      reads=[xttok[xi], ptok(gn, gh)], writes=[f"xs{op_}"])
                dve_flush(x_st[op_][0:1, 0:1], [f"xs{op_}"])
                if tt < 4:
                    t2 = tt + 4
                    dma("sync", xtb[xi][:], xt_d[t2 * P:(t2 + 1) * P, ec * TH:(ec + 1) * TH],
                        [xttok[xi]], [xttok[xi]])
                dma("gpsimd", out_d[tt * P:(tt + 1) * P, ec * TH:(ec + 1) * TH], x_st[op_][:],
                    [f"xs{op_}"], [f"out.{ec}.{tt}"])

        if debug:
            all_xnf = [f"xnf.{c}.{h}" for c in range(C) for h in range(2)]
            all_xqf = [f"xqf.{c}.{h}" for c in range(C) for h in range(2)]
            all_phiq = [f"phiq.{c}.{h}" for c in range(C) for h in range(2)]
            dma("gpsimd", dbg_d["kv"][:, :], kv[:], [f"kv.{c}" for c in range(C)], ["dbgkv"])
            dma("gpsimd", dbg_d["phiq"][:, :], phiq[:], all_phiq, ["dbgpq"])
            dma("gpsimd", dbg_d["xqf"][:, :], xqf[:], all_xqf, ["dbgxq"])
            dma("gpsimd", dbg_d["ctx"][:, :], ctxf[:], all_xnf, ["dbgctx"])
            dma("gpsimd", dbg_gl[:, :], gl_s[:], ["gl"], ["dbggl"])
            dma("gpsimd", dbg_S[:, :], S_s[:], ["S"], ["dbgS"])
            dma("gpsimd", dbg_ctb[:, :], ctb[:], ["ctb"], ["dbgctb"])

        if max_ops is not None:
            del S.ops[max_ops:]
        cum = S.resolve()
        if info is not None:
            info["n_ops"] = len(S.ops)
            info["cum"] = dict(cum)
            info["n_waits"] = sum(len(o["waits"]) for o in S.ops)
            info["n_incs"] = sum(1 for o in S.ops if o["need"])

        semkeys = sorted({op["key"] for op in S.ops} |
                         {k for op in S.ops for k in op["waits"]})
        sems = {k: es.enter_context(nc.semaphore(f"sem_{k[0]}_{k[1]}")) for k in semkeys}

        with nc.Block() as block:
            def emit_for(name, proxy):
                for op in S.ops:
                    if op["eng"] != name:
                        continue
                    for k, v in sorted(op["waits"].items()):
                        proxy.wait_ge(sems[k], v)
                    inst = op["fn"](proxy)
                    if op["need"]:
                        inst.then_inc(sems[op["key"]], 16 if op["dma"] else 1)

            @block.sync
            def _(sync):
                emit_for("sync", sync)

            @block.gpsimd
            def _(g):
                emit_for("gpsimd", g)
                if ("gpsimd", "d") in sems:
                    g.wait_ge(sems[("gpsimd", "d")], cum[("gpsimd", "d")])

            @block.vector
            def _(v):
                emit_for("vector", v)

            @block.scalar
            def _(s):
                emit_for("scalar", s)

            @block.tensor
            def _(t):
                emit_for("tensor", t)

    return nc


def _mask_for(core):
    m = np.zeros(NCORES, np.float32)
    for r in range(NCORES):
        if r // 4 == core // 4 and r % 4 < core % 4:
            m[r] = 1.0
    return m


def kernel(**inputs):
    asc = np.ascontiguousarray
    f = lambda k: np.asarray(inputs[k], dtype=np.float32)
    x = f("L0_attn_x")
    WqT, WkT, WvT, WoT = (asc(f(k).T) for k in ("L0_attn_Wq", "L0_attn_Wk", "L0_attn_Wv", "L0_attn_Wo"))
    WfT = asc(f("L0_ffn_weight").T)
    RrT = asc(f("L0_ffn_rram_variation").T)
    bq, bk, bv, bo = (f(k) for k in ("L0_attn_bq", "L0_attn_bk", "L0_attn_bv", "L0_attn_bo"))

    blk_qkv = lambda WT: asc(WT.reshape(C, P, C, P).transpose(2, 1, 0, 3))
    blk_g = lambda WT: asc(WT.reshape(C, P, NEC, TH).transpose(2, 1, 0, 3))
    wq4, wk4, wv4 = blk_qkv(WqT), blk_qkv(WkT), blk_qkv(WvT)
    wo4, wf4, rr4 = blk_g(WoT), blk_g(WfT), blk_g(RrT)
    bcol = lambda b: asc(b.reshape(C, P).T)
    bq_f, bk_f, bv_f = bcol(bq), bcol(bk), bcol(bv)
    bo_r = asc(bo.reshape(1, D))

    nc = build_nc()

    in_maps = []
    for core in range(NCORES):
        b, j = core // 4, core % 4
        xt = asc(x[b, j * TLOC:(j + 1) * TLOC, :])
        xf = asc(xt.T)
        maskb = asc(np.broadcast_to(np.repeat(_mask_for(core), C)[None, :], (P, NCORES * C)))
        in_maps.append(dict(
            xt=xt, xf=xf, wq4=wq4, wk4=wk4, wv4=wv4, wo4=wo4, wf4=wf4, rr4=rr4,
            bq=bq_f, bk=bk_f, bv=bv_f, bo=bo_r, maskb=maskb))

    res = run_bass_kernel_spmd(nc, in_maps, core_ids=list(range(NCORES)))

    out = np.empty((BATCH, L, D), np.float32)
    for core in range(NCORES):
        b, j = core // 4, core % 4
        out[b, j * TLOC:(j + 1) * TLOC, :] = res.results[core]["out"]
    return out


# version 13
# speedup vs baseline: 1.0414x; 1.0414x over previous
import numpy as np
from contextlib import ExitStack

import concourse.bass as bass
import concourse.mybir as mybir
from concourse.bass_utils import run_bass_kernel_spmd

AF = mybir.ActivationFunctionType
OP = mybir.AluOpType
F32 = mybir.dt.float32
BF16 = mybir.dt.bfloat16

NCORES = 8
BATCH = 2
L = 4096
D = 2048
P = 128
C = D // P
TLOC = L * BATCH // NCORES
TH = 512
NEC = 4
SCALE = float(P) ** -0.5
EPS = 1e-5

CN_M33, CN_P33, CN_M02, CN_P02, CN_EPS, CN_MH = 0, 1, 2, 3, 4, 5
ENGINES = ("sync", "gpsimd", "vector", "scalar", "tensor")


class Sched:

    def __init__(self):
        self.ops = []
        self.last_write = {}
        self.readers = {}
        self.bar = set()
        self.bar_eng = ()

    def barrier(self, engines=ENGINES, only=()):
        self.bar_eng = only
        last = {}
        for op in self.ops:
            if op["eng"] in engines:
                last[op["eng"]] = op["i"]
        self.bar = set(last.values())

    def add(self, eng, fn, reads=(), writes=(), dma=False, cc=False):
        deps = set(self.bar) if (not self.bar_eng or eng in self.bar_eng) else set()
        for t in reads:
            w = self.last_write.get(t)
            if w is not None:
                deps.add(w)
        for t in writes:
            w = self.last_write.get(t)
            if w is not None:
                deps.add(w)
            for r in self.readers.get(t, ()):
                deps.add(r)
        i = len(self.ops)
        self.ops.append(dict(i=i, eng=eng, fn=fn, dma=dma, cc=cc,
                             deps=deps, need=False, val=None, waits=None))
        for t in reads:
            self.readers.setdefault(t, []).append(i)
        for t in writes:
            self.last_write[t] = i
            self.readers[t] = []
        return i

    def resolve(self):
        ops = self.ops

        def cat(op):
            return "x" if op["cc"] else ("d" if op["dma"] else "c")

        for op in ops:
            best = {}
            for d in op["deps"]:
                k = (ops[d]["eng"], cat(ops[d]))
                if d > best.get(k, -1):
                    best[k] = d
            op["deps"] = set(best.values())
        for op in ops:
            if op["dma"] or op["cc"]:
                op["need"] = True
        for op in ops:
            for d in op["deps"]:
                dep = ops[d]
                if dep["eng"] == op["eng"] and cat(dep) == "c":
                    continue
                dep["need"] = True
        keys = [(e, c) for e in ENGINES for c in ("c", "d")] + [("gpsimd", "x")]
        cum = {k: 0 for k in keys}
        for op in ops:
            k = (op["eng"], cat(op))
            if op["need"]:
                cum[k] += 16 if op["dma"] else 1
            op["val"] = cum[k]
            op["key"] = k
        vc = {e: {k: 0 for k in keys} for e in ENGINES}
        snap = {}
        for op in ops:
            E = op["eng"]
            waits = {}
            for d in sorted(op["deps"]):
                dep = ops[d]
                if dep["eng"] == E and cat(dep) == "c":
                    continue
                k, v = dep["key"], dep["val"]
                if vc[E][k] >= v:
                    continue
                waits[k] = max(waits.get(k, 0), v)
            for k, v in waits.items():
                if v > vc[E][k]:
                    vc[E][k] = v
                s = snap.get((k, v))
                if s:
                    for k3, v3 in s.items():
                        if v3 > vc[E][k3]:
                            vc[E][k3] = v3
            op["waits"] = waits
            if op["need"]:
                s = dict(vc[E])
                s[op["key"]] = op["val"]
                snap[(op["key"], op["val"])] = s
        return cum


def build_nc(max_ops=None, info=None, debug=False, barriers=(), cast_eng="gpsimd"):
    nc = bass.Bass("TRN2", target_bir_lowering=False)
    if debug:
        dbg_d = {n: nc.dram_tensor(f"dbg_{n}", [P, C * TLOC], BF16, kind="ExternalOutput")
                 for n in ("kv", "phiq", "xqf", "ctx", "xn")}
        dbg_gl = nc.dram_tensor("dbg_gl", [P, C], F32, kind="ExternalOutput")
        dbg_u = nc.dram_tensor("dbg_u", [P, TLOC], BF16, kind="ExternalOutput")
        dbg_pk = nc.dram_tensor("dbg_pk", [P, TLOC], BF16, kind="ExternalOutput")
        dbg_S = nc.dram_tensor("dbg_S", [P, NCORES * C], F32, kind="ExternalOutput")
        dbg_ctb = nc.dram_tensor("dbg_ctb", [P, C], F32, kind="ExternalOutput")

    xt_d = nc.dram_tensor("xt", [TLOC, D], F32, kind="ExternalInput")
    xf_d = nc.dram_tensor("xf", [D, TLOC], F32, kind="ExternalInput")
    wq4_d = nc.dram_tensor("wq4", [C, P, C, P], F32, kind="ExternalInput")
    wk4_d = nc.dram_tensor("wk4", [C, P, C, P], F32, kind="ExternalInput")
    wv4_d = nc.dram_tensor("wv4", [C, P, C, P], F32, kind="ExternalInput")
    wo4_d = nc.dram_tensor("wo4", [NEC, P, C, TH], F32, kind="ExternalInput")
    wf4_d = nc.dram_tensor("wf4", [NEC, P, C, TH], F32, kind="ExternalInput")
    rr4_d = nc.dram_tensor("rr4", [NEC, P, C, TH], F32, kind="ExternalInput")
    bq_d = nc.dram_tensor("bq", [P, C], F32, kind="ExternalInput")
    bk_d = nc.dram_tensor("bk", [P, C], F32, kind="ExternalInput")
    bv_d = nc.dram_tensor("bv", [P, C], F32, kind="ExternalInput")
    bo_d = nc.dram_tensor("bo", [1, D], F32, kind="ExternalInput")
    maskb_d = nc.dram_tensor("maskb", [P, NCORES * C], F32, kind="ExternalInput")
    out_d = nc.dram_tensor("out", [TLOC, D], F32, kind="ExternalOutput")

    in_cc = nc.dram_tensor("in_cc", [1, D], F32)
    out_cc = nc.dram_tensor("out_cc", [NCORES, D], F32)

    es = ExitStack()
    sb = lambda n, s, d: es.enter_context(nc.sbuf_tensor(n, s, d))
    pt = lambda n, s: es.enter_context(nc.psum_tensor(n, s, F32))

    with es:
        xnf = sb("xnf", [P, C * TLOC], BF16)
        xqf = sb("xqf", [P, C * TLOC], BF16)
        phiq = sb("phiq", [P, C * TLOC], BF16)
        kv = sb("kv", [P, C * TLOC], BF16)

        wst = sb("wst", [P, C * P], F32)
        wkb = [sb("wkb0", [P, C * P], BF16), sb("wkb1", [P, C * P], BF16)]
        wvb = [sb("wvb0", [P, C * P], BF16), sb("wvb1", [P, C * P], BF16)]
        gst_r = [sb("gstr0", [P, TH], F32), sb("gstr1", [P, TH], F32)]

        x_st = [sb(f"x_st{i}", [P, TH], F32) for i in range(4)]
        x2b = [sb(f"x2b{i}", [P, TH], BF16) for i in range(2)]
        xn_a = sb("xn_a", [P, TH], F32)
        xn_b = [sb(f"xn_b{i}", [P, TH], F32) for i in range(2)]
        s1_t = [sb(f"s1t{i}", [P, TH], BF16) for i in range(2)]
        s2_t = [sb(f"s2t{i}", [P, TH], BF16) for i in range(2)]
        tq_t = sb("tq_t", [P, TH], BF16)
        e_t = [sb(f"et{i}", [P, TH], F32) for i in range(2)]
        a_t = sb("a_t", [P, TH], F32)
        phik = sb("phik", [P, TLOC], BF16)
        u_t = sb("u_t", [P, TLOC], BF16)
        ones_bf = sb("ones_bf", [P, TLOC], BF16)

        rows = sb("rows", [1, 2560], F32)
        bo_b = [sb(f"bo_b{i}", [1, TH], BF16) for i in range(2)]

        bq_s = sb("bq_s", [P, C], F32)
        bk_s = sb("bk_s", [P, C], F32)
        bv_s = sb("bv_s", [P, C], F32)
        bvS = sb("bvS", [P, C], F32)
        maskb_s = sb("maskb_s", [P, NCORES * C], F32)
        S_s = sb("S_s", [P, NCORES * C], F32)
        masked_s = sb("masked_s", [P, NCORES * C], F32)
        gl_s = sb("gl_s", [P, C], F32)
        ctb = sb("ctb", [P, C], F32)

        ones_col = sb("ones_col", [P, 1], F32)
        ones_colb = sb("ones_colb", [P, 1], BF16)
        ones_rowf = sb("ones_rowf", [1, P], F32)
        ones_rowb = sb("ones_rowb", [1, P], BF16)
        consts = sb("consts", [P, 8], F32)
        fl_scr = sb("fl_scr", [1, 4], F32)

        psn = ["psa", "psb", "psc", "psd"]
        pst = {n: pt(n, [P, 2 * TH]) for n in psn}
        half = lambda n, h: pst[n][:, h * TH:(h + 1) * TH]
        ptok = lambda n, h: f"{n}{h}"
        psG = [(n, h) for n in psn for h in range(2)]

        S = Sched()

        def dve_flush(src_ap, tokens):
            S.add("vector", lambda e, a=src_ap: e.tensor_copy(fl_scr[0:1, 0:1], a),
                  reads=list(tokens), writes=list(tokens))

        def dma(eng, out, in_, reads, writes):
            S.add(eng, lambda e, o=out, i=in_: e.dma_start(out=o, in_=i),
                  reads=reads, writes=writes, dma=True)

        S.add("gpsimd", lambda e: e.memset(ones_col[:], 1.0), writes=["onc"])
        S.add("gpsimd", lambda e: e.memset(ones_colb[:], 1.0), writes=["oncb"])
        S.add("gpsimd", lambda e: e.memset(ones_rowf[:], 1.0), writes=["onrf"])
        S.add("gpsimd", lambda e: e.memset(ones_rowb[:], 1.0), writes=["onrb"])
        S.add("gpsimd", lambda e: e.memset(ones_bf[:], 1.0), writes=["onbf"])
        for col, val in [(CN_M33, -0.33), (CN_P33, 0.33), (CN_M02, -0.2),
                         (CN_P02, 0.2), (CN_EPS, EPS), (CN_MH, -0.5)]:
            S.add("gpsimd", lambda e, c=col, v=val: e.memset(consts[:, c:c + 1], v),
                  writes=[f"cn{col}"])
        dma("sync", bq_s[:], bq_d[:, :], [], ["bq"])
        dma("sync", bk_s[:], bk_d[:, :], [], ["bk"])
        dma("sync", bv_s[:], bv_d[:, :], [], ["bv"])
        dma("sync", maskb_s[:], maskb_d[:, :], [], ["maskb"])
        S.add("vector", lambda e: e.tensor_scalar(bvS[:], bv_s[:], SCALE, None, OP.mult),
              reads=["bv"], writes=["bvS"])

        def load_qkv_weight(wsrc, dstb, dsttok, scale=None, eng=None):
            dma("sync", wst[:], wsrc, ["wst.0", "wst.1", "wst.2", "wst.3"],
                ["wst.0", "wst.1", "wst.2", "wst.3"])
            if scale is None:
                e_ = eng or cast_eng
                S.add(e_, lambda e, d=dstb: e.tensor_copy(d[:], wst[:]),
                      reads=["wst.0", "wst.1", "wst.2", "wst.3"], writes=[dsttok])
                if e_ == "vector":
                    dve_flush(dstb[0:1, 0:1], [dsttok])
            else:
                S.add("vector", lambda e, d=dstb, sc=scale: e.tensor_scalar(d[:], wst[:], sc, None, OP.mult),
                      reads=["wst.0", "wst.1", "wst.2", "wst.3"], writes=[dsttok])
                dve_flush(dstb[0:1, 0:1], [dsttok])

        def load_b(c):
            p = c % 2
            load_qkv_weight(wk4_d[c], wkb[p], f"wkb{p}.o", None)
            load_qkv_weight(wv4_d[c], wvb[p], f"wvb{p}", SCALE)

        load_b(0)
        load_b(1)

        mrow = rows[0:1, 0:TH]
        vrow = rows[0:1, TH:2 * TH]
        trow = rows[0:1, 2 * TH:3 * TH]
        lrow = rows[0:1, 3 * TH:4 * TH]
        rrow = rows[0:1, 4 * TH:5 * TH]
        for h in range(2):
            t0 = h * TH
            pss = psn[2 * h]
            for dc in range(C):
                xr = x_st[dc % 2]
                xb = x2b[dc % 2]
                xtok = f"xs{dc % 2}"
                dma("gpsimd", xr[:], xf_d[dc * P:(dc + 1) * P, t0:t0 + TH], [xtok], [xtok])
                S.add("scalar", lambda e, a=xb, b=xr: e.activation(a[:], b[:], AF.Square),
                      reads=[xtok], writes=[f"x2b{dc % 2}"])
                S.add("tensor", lambda e, b=xr, s=(dc == 0), p=(dc == C - 1), ps=pss:
                      e.matmul(half(ps, 0)[0:1, :], ones_col[:], b[:], start=s, stop=p,
                               skip_group_check=True),
                      reads=[xtok, "onc"], writes=[ptok(pss, 0)])
                S.add("tensor", lambda e, b=xb, s=(dc == 0), p=(dc == C - 1), ps=pss:
                      e.matmul(half(ps, 1)[0:1, :], ones_colb[:], b[:], start=s, stop=p,
                               skip_group_check=True),
                      reads=[f"x2b{dc % 2}", "oncb"], writes=[ptok(pss, 1)])
        for h in range(2):
            pss = psn[2 * h]
            S.add("vector", lambda e, ps=pss: e.tensor_scalar(mrow, half(ps, 0)[0:1, :], 1.0 / D, None, OP.mult),
                  reads=[ptok(pss, 0)], writes=["rows"])
            S.add("vector", lambda e, ps=pss: e.tensor_scalar(vrow, half(ps, 1)[0:1, :], 1.0 / D, None, OP.mult),
                  reads=[ptok(pss, 1)], writes=["rows"])
            S.add("scalar", lambda e: e.activation(trow, mrow, AF.Square),
                  reads=["rows"], writes=["rows"])
            S.add("vector", lambda e: e.tensor_tensor(lrow, vrow, trow, OP.subtract),
                  reads=["rows"], writes=["rows"])
            S.add("scalar", lambda e: e.activation(trow, lrow, AF.Ln,
                                                   bias=consts[0:1, CN_EPS:CN_EPS + 1], scale=1.0),
                  reads=["rows", f"cn{CN_EPS}"], writes=["rows"])
            S.add("scalar", lambda e: e.activation(rrow, trow, AF.Exp, bias=0.0,
                                                   scale=consts[0:1, CN_MH:CN_MH + 1]),
                  reads=["rows", f"cn{CN_MH}"], writes=["rows"])
            S.add("tensor", lambda e, hh=h: e.matmul(half("psb", hh), ones_rowf[:], mrow,
                                                     start=True, stop=True, skip_group_check=True),
                  reads=["rows", "onrf"], writes=[ptok("psb", h)])
            S.add("tensor", lambda e, hh=h: e.matmul(half("psd", hh), ones_rowf[:], rrow,
                                                     start=True, stop=True, skip_group_check=True),
                  reads=["rows", "onrf"], writes=[ptok("psd", h)])
        for dc in range(C):
            for h in range(2):
                t0 = h * TH
                xr = x_st[2 + h]
                xtok = f"xs{2 + h}"
                nb = xn_b[h]
                nbt = f"xnb{h}"
                sl = slice(dc * TLOC + t0, dc * TLOC + t0 + TH)
                dma("gpsimd", xr[:], xf_d[dc * P:(dc + 1) * P, t0:t0 + TH], [xtok], [xtok])
                S.add("vector", lambda e, b=xr, hh=h: e.tensor_tensor(xn_a[:], b[:], half("psb", hh), OP.subtract),
                      reads=[xtok, ptok("psb", h)], writes=["xna"])
                S.add("vector", lambda e, n=nb, hh=h: e.tensor_tensor(n[:], xn_a[:], half("psd", hh), OP.mult),
                      reads=["xna", ptok("psd", h)], writes=[nbt])
                S.add("vector", lambda e, n=nb, s=sl: e.tensor_copy(xnf[:, s], n[:]),
                      reads=[nbt], writes=[f"xnf.{dc}.{h}"])
                dve_flush(xnf[0:1, sl.start:sl.start + 1], [f"xnf.{dc}.{h}"])
                S.add("scalar", lambda e, n=nb, t=s1_t[h]: e.activation(
                    t[:], n[:], AF.Sign, bias=consts[:, CN_M33:CN_M33 + 1], scale=1.0),
                    reads=[nbt, f"cn{CN_M33}"], writes=[f"s1{h}"])
                S.add("scalar", lambda e, n=nb, t=s2_t[h]: e.activation(
                    t[:], n[:], AF.Sign, bias=consts[:, CN_P33:CN_P33 + 1], scale=1.0),
                    reads=[nbt, f"cn{CN_P33}"], writes=[f"s2{h}"])
                S.add("vector", lambda e, u=s1_t[h], v=s2_t[h]: e.tensor_tensor(
                    tq_t[:], u[:], v[:], OP.add),
                    reads=[f"s1{h}", f"s2{h}"], writes=["tq"])
                S.add("vector", lambda e, s=sl: e.tensor_scalar(xqf[:, s], tq_t[:], 0.25, None, OP.mult),
                      reads=["tq"], writes=[f"xqf.{dc}.{h}"])
                dve_flush(xqf[0:1, sl.start:sl.start + 1], [f"xqf.{dc}.{h}"])

        xnf_all = [f"xnf.{dc}.{h}" for dc in range(C) for h in range(2)]
        for b in barriers:
            if b == "AB":
                S.barrier()
            elif b.startswith("AB:"):
                spec = b.split(":")[1]
                if "@" in spec:
                    src, dst = spec.split("@")
                    S.barrier(tuple(src.split("+")), only=tuple(dst.split("+")))
                else:
                    S.barrier(tuple(spec.split("+")))

        def qkv_mm(psname, wb, wtok, c, dc, h):
            xsl = slice(dc * TLOC + h * TH, dc * TLOC + (h + 1) * TH)
            rds = [wtok, f"xnf.{dc}.{h}"]
            if c == 0:
                rds.append(f"xqf.{dc}.{h}")
            S.add("tensor", lambda e, w=wb, d=dc, x=xsl, ps=psname, hh=h,
                  s=(dc == 0), p=(dc == C - 1):
                  e.matmul(half(ps, hh), w[:, d * P:(d + 1) * P], xnf[:, x],
                           start=s, stop=p, skip_group_check=True),
                  reads=rds, writes=[ptok(psname, h)])

        def phi_block(psname, bias_s, btok, c, outbuf, base, outtok_fn):
            for h in range(2):
                et = e_t[h]
                S.add("scalar", lambda e, t=et, ps=psname, hh=h, cc=c, b=bias_s:
                      e.activation(t[:], half(ps, hh), AF.Exp, bias=b[:, cc:cc + 1], scale=1.0),
                      reads=[ptok(psname, h), btok], writes=[f"et{h}"])
                S.add("vector", lambda e, ps=psname, hh=h, cc=c, b=bias_s:
                      e.tensor_scalar(a_t[:], half(ps, hh), b[:, cc:cc + 1], 0.0, OP.add, OP.max),
                      reads=[ptok(psname, h), btok, f"et{h}"], writes=["at"])
                S.add("vector", lambda e, t=et, o=outbuf, sl=slice(base + h * TH, base + (h + 1) * TH):
                      e.scalar_tensor_tensor(o[:, sl], t[:], 1.0, a_t[:], OP.min, OP.add),
                      reads=[f"et{h}", "at"], writes=[outtok_fn(h)])

        for c in range(C):
            p = c % 2
            kps = psn[2 * (c % 2)]
            vps = psn[2 * (c % 2) + 1]
            if c == 0:
                vps = "psc"
            elif c == 1:
                kps = "psb"
            if 1 <= c < C - 1:
                load_b(c + 1)
            for dc in range(C):
                for h in range(2):
                    qkv_mm(kps, wkb[p], f"wkb{p}.o", c, dc, h)
                    qkv_mm(vps, wvb[p], f"wvb{p}", c, dc, h)
            phi_block(kps, bk_s, "bk", c, phik, 0, lambda h: f"phik{h}")
            for h in range(2):
                S.add("vector", lambda e, ps=vps, hh=h, cc=c,
                      sl=slice(h * TH, (h + 1) * TH):
                      e.scalar_tensor_tensor(u_t[:, sl], half(ps, hh), bvS[:, cc:cc + 1],
                                             phik[:, sl], OP.add, OP.mult),
                      reads=[ptok(vps, h), "bvS", f"phik{h}"], writes=[f"u{h}"])
            S.add("vector", lambda e, cc=c: e.tensor_tensor_scan(
                kv[:, cc * TLOC:(cc + 1) * TLOC], u_t[:], ones_bf[:], 0.0, OP.add, OP.mult),
                reads=["u0", "u1", "onbf"], writes=[f"kv.{c}"])

        def emit_collective():
            if "CX" in barriers:
                S.barrier()
            kvv = kv.ap().rearrange("p (c t) -> p c t", c=C, t=TLOC)
            ctbv = ctb.ap().rearrange("p (c o) -> p c o", c=C, o=1)
            kv_all = [f"kv.{c}" for c in range(C)]
            S.add("vector", lambda e: e.tensor_copy(ctbv, kvv[:, :, TLOC - 1:TLOC]),
                  reads=kv_all, writes=["ctb"])
            dve_flush(ctb[0:1, 0:1], ["ctb"])
            S.add("gpsimd", lambda e: e.dma_start(
                out=in_cc.ap().rearrange("r (p c) -> p r c", p=P, c=C),
                in_=ctb.ap().rearrange("p (r c) -> p r c", r=1, c=C)),
                reads=["ctb"], writes=["in_cc"], dma=True)
            S.add("gpsimd", lambda e: e.collective_compute(
                "AllGather", OP.bypass, replica_groups=[list(range(NCORES))],
                ins=[in_cc[:, :]], outs=[out_cc[:, :]]),
                reads=["in_cc"], writes=["out_cc"], cc=True)
            S.add("gpsimd", lambda e: e.dma_start(
                out=S_s.ap().rearrange("p (r c) -> p r c", r=NCORES, c=C),
                in_=out_cc.ap().rearrange("r (p c) -> p r c", p=P, c=C)),
                reads=["out_cc"], writes=["S"], dma=True)

            if debug:
                all_xnf0 = [f"xnf.{c}.{h}" for c in range(C) for h in range(2)]
                dma("gpsimd", dbg_d["xn"][:, :], xnf[:], all_xnf0, ["dbgxn"])
                dma("gpsimd", dbg_u[:, :], u_t[:], ["u0", "u1"], ["dbgu"])
                dma("gpsimd", dbg_pk[:, :], phik[:], ["phik0", "phik1"], ["dbgpk"])

        if "BC" in barriers:
            S.barrier()
        def emit_collective_finish():
            S.add("vector", lambda e: e.tensor_mul(masked_s[:], S_s[:], maskb_s[:]),
                  reads=["S", "maskb"], writes=["masked"])
            S.add("vector", lambda e: e.reduce_sum(
                gl_s[:], masked_s.ap().rearrange("p (r c) -> p c r", r=NCORES, c=C),
                axis=mybir.AxisListType.X),
                reads=["masked"], writes=["gl"])

        def load_cq(c):
            load_qkv_weight(wq4_d[c], wkb[c % 2], f"wkb{c % 2}.o", None, eng="vector")

        load_cq(0)
        for c in range(C):
            p = c % 2
            qps = psn[2 * (c % 2)]
            if c + 1 < C:
                load_cq(c + 1)
            for dc in range(C):
                for h in range(2):
                    qkv_mm(qps, wkb[p], f"wkb{p}.o", c, dc, h)
            phi_block(qps, bq_s, "bq", c, phiq, c * TLOC, lambda h, cc=c: f"phiq.{cc}.{h}")
            if c == 2:
                emit_collective()
        emit_collective_finish()

        ctxf = xnf

        def ctx_emit(c):
            sl = slice(c * TLOC, (c + 1) * TLOC)
            S.add("vector", lambda e, s=sl, cc=c: e.scalar_tensor_tensor(
                ctxf[:, s], kv[:, s], gl_s[:, cc:cc + 1], phiq[:, s], OP.add, OP.mult),
                reads=[f"kv.{c}", "gl", f"phiq.{c}.0", f"phiq.{c}.1"],
                writes=[f"xnf.{c}.0", f"xnf.{c}.1"])
            dve_flush(ctxf[0:1, c * TLOC:c * TLOC + 1], [f"xnf.{c}.0", f"xnf.{c}.1"])

        if "XG" in barriers:
            S.barrier()
        dma("sync", rows[0:1, 0:4 * TH], bo_d[:, :], ["rows"], ["rows"])
        wob = [wkb[0][:, 0:TH], wkb[1][:, 0:TH]]
        wfb = [wkb[0][:, TH:2 * TH], wkb[1][:, TH:2 * TH]]
        gst_o = [wst[:, 0:TH], wst[:, TH:2 * TH]]
        gst_f = [wst[:, 2 * TH:3 * TH], wst[:, 3 * TH:4 * TH]]
        xtb = [x_st[0], x_st[1], xn_b[0], xn_b[1]]
        xttok = ["xs0", "xs1", "xnb0", "xnb1"]

        def load_g(ec, dc):
            p = dc % 2
            dma("sync", gst_o[p], wo4_d[ec][:, dc, :], [f"wst.{p}"], [f"wst.{p}"])
            S.add(cast_eng, lambda e, w=wob[p], g=gst_o[p]: e.tensor_copy(w[:], g[:]),
                  reads=[f"wst.{p}"], writes=[f"wkb{p}.o"])
            dma("sync", gst_f[p], wf4_d[ec][:, dc, :], [f"wst.{2 + p}"], [f"wst.{2 + p}"])
            dma("sync", gst_r[p][:], rr4_d[ec][:, dc, :], [f"gstr{p}"], [f"gstr{p}"])
            S.add("scalar", lambda e, g=gst_f[p], t=s1_t[p]: e.activation(
                t[:], g[:], AF.Sign, bias=consts[:, CN_M02:CN_M02 + 1], scale=1.0),
                reads=[f"wst.{2 + p}", f"cn{CN_M02}"], writes=[f"s1{p}"])
            S.add("scalar", lambda e, g=gst_f[p], t=s2_t[p]: e.activation(
                t[:], g[:], AF.Sign, bias=consts[:, CN_P02:CN_P02 + 1], scale=1.0),
                reads=[f"wst.{2 + p}", f"cn{CN_P02}"], writes=[f"s2{p}"])
            S.add("vector", lambda e, u=s1_t[p], v=s2_t[p]: e.tensor_tensor(
                tq_t[:], u[:], v[:], OP.add),
                reads=[f"s1{p}", f"s2{p}"], writes=["tq"])
            S.add("vector", lambda e, w=wfb[p], g=gst_r[p]: e.scalar_tensor_tensor(
                w[:], tq_t[:], 0.3, g[:], OP.mult, OP.add),
                reads=["tq", f"gstr{p}"], writes=[f"wkb{p}.f", f"wkb{p}.o"])
            dve_flush(wfb[p][0:1, 0:1], [f"wkb{p}.f"])

        ctx_emit(0)
        ctx_emit(1)
        load_g(0, 0)
        for ec in range(NEC):
            bop = ec % 2
            S.add("gpsimd", lambda e, c=ec, b=bo_b[bop]: e.tensor_copy(
                b[:], rows[0:1, c * TH:(c + 1) * TH]),
                reads=["rows"], writes=[f"bob{bop}"])
            for dc in range(C):
                p = dc % 2
                if ec == 0 and dc + 2 < C:
                    ctx_emit(dc + 2)
                if dc + 1 < C:
                    load_g(ec, dc + 1)
                elif ec + 1 < NEC:
                    load_g(ec + 1, 0)
                if 12 <= dc:
                    tt = dc - 12
                    dma("sync", xtb[tt][:], xt_d[tt * P:(tt + 1) * P, ec * TH:(ec + 1) * TH],
                        [xttok[tt]], [xttok[tt]])
                for tt in range(8):
                    hh = tt // 4
                    csl = slice(dc * TLOC + tt * P, dc * TLOC + (tt + 1) * P)
                    gn, gh = psG[tt]
                    S.add("tensor", lambda e, x=csl, w=wob[p], n=gn, g=gh, s=(dc == 0):
                          e.matmul(half(n, g), ctxf[:, x], w[:], start=s, stop=False,
                                   skip_group_check=True),
                          reads=[f"xnf.{dc}.{hh}", f"wkb{p}.o"], writes=[ptok(gn, gh)])
                    S.add("tensor", lambda e, x=csl, w=wfb[p], n=gn, g=gh:
                          e.matmul(half(n, g), xqf[:, x], w[:], start=False, stop=False,
                                   skip_group_check=True),
                          reads=[f"xqf.{dc}.{hh}", f"wkb{p}.f"], writes=[ptok(gn, gh)])
            for tt in range(8):
                gn, gh = psG[tt]
                xi = tt % 4
                op_ = 2 + tt % 2
                S.add("tensor", lambda e, n=gn, g=gh, b=bo_b[bop]:
                      e.matmul(half(n, g), ones_rowb[:], b[:], start=False, stop=True,
                               skip_group_check=True),
                      reads=[f"bob{bop}", "onrb"], writes=[ptok(gn, gh)])
                S.add("vector", lambda e, n=gn, g=gh, a=xtb[xi], b=x_st[op_]:
                      e.tensor_tensor(b[:], a[:], half(n, g), OP.add),
                      reads=[xttok[xi], ptok(gn, gh)], writes=[f"xs{op_}"])
                dve_flush(x_st[op_][0:1, 0:1], [f"xs{op_}"])
                if tt < 4:
                    t2 = tt + 4
                    dma("sync", xtb[xi][:], xt_d[t2 * P:(t2 + 1) * P, ec * TH:(ec + 1) * TH],
                        [xttok[xi]], [xttok[xi]])
                dma("gpsimd", out_d[tt * P:(tt + 1) * P, ec * TH:(ec + 1) * TH], x_st[op_][:],
                    [f"xs{op_}"], [f"out.{ec}.{tt}"])

        if debug:
            all_xnf = [f"xnf.{c}.{h}" for c in range(C) for h in range(2)]
            all_xqf = [f"xqf.{c}.{h}" for c in range(C) for h in range(2)]
            all_phiq = [f"phiq.{c}.{h}" for c in range(C) for h in range(2)]
            dma("gpsimd", dbg_d["kv"][:, :], kv[:], [f"kv.{c}" for c in range(C)], ["dbgkv"])
            dma("gpsimd", dbg_d["phiq"][:, :], phiq[:], all_phiq, ["dbgpq"])
            dma("gpsimd", dbg_d["xqf"][:, :], xqf[:], all_xqf, ["dbgxq"])
            dma("gpsimd", dbg_d["ctx"][:, :], ctxf[:], all_xnf, ["dbgctx"])
            dma("gpsimd", dbg_gl[:, :], gl_s[:], ["gl"], ["dbggl"])
            dma("gpsimd", dbg_S[:, :], S_s[:], ["S"], ["dbgS"])
            dma("gpsimd", dbg_ctb[:, :], ctb[:], ["ctb"], ["dbgctb"])

        if max_ops is not None:
            del S.ops[max_ops:]
        cum = S.resolve()
        if info is not None:
            info["n_ops"] = len(S.ops)
            info["cum"] = dict(cum)
            info["n_waits"] = sum(len(o["waits"]) for o in S.ops)
            info["n_incs"] = sum(1 for o in S.ops if o["need"])

        semkeys = sorted({op["key"] for op in S.ops} |
                         {k for op in S.ops for k in op["waits"]})
        sems = {k: es.enter_context(nc.semaphore(f"sem_{k[0]}_{k[1]}")) for k in semkeys}

        with nc.Block() as block:
            def emit_for(name, proxy):
                for op in S.ops:
                    if op["eng"] != name:
                        continue
                    for k, v in sorted(op["waits"].items()):
                        proxy.wait_ge(sems[k], v)
                    inst = op["fn"](proxy)
                    if op["need"]:
                        inst.then_inc(sems[op["key"]], 16 if op["dma"] else 1)

            @block.sync
            def _(sync):
                emit_for("sync", sync)

            @block.gpsimd
            def _(g):
                emit_for("gpsimd", g)
                if ("gpsimd", "d") in sems:
                    g.wait_ge(sems[("gpsimd", "d")], cum[("gpsimd", "d")])

            @block.vector
            def _(v):
                emit_for("vector", v)

            @block.scalar
            def _(s):
                emit_for("scalar", s)

            @block.tensor
            def _(t):
                emit_for("tensor", t)

    return nc


def _mask_for(core):
    m = np.zeros(NCORES, np.float32)
    for r in range(NCORES):
        if r // 4 == core // 4 and r % 4 < core % 4:
            m[r] = 1.0
    return m


def kernel(**inputs):
    asc = np.ascontiguousarray
    f = lambda k: np.asarray(inputs[k], dtype=np.float32)
    x = f("L0_attn_x")
    WqT, WkT, WvT, WoT = (asc(f(k).T) for k in ("L0_attn_Wq", "L0_attn_Wk", "L0_attn_Wv", "L0_attn_Wo"))
    WfT = asc(f("L0_ffn_weight").T)
    RrT = asc(f("L0_ffn_rram_variation").T)
    bq, bk, bv, bo = (f(k) for k in ("L0_attn_bq", "L0_attn_bk", "L0_attn_bv", "L0_attn_bo"))

    blk_qkv = lambda WT: asc(WT.reshape(C, P, C, P).transpose(2, 1, 0, 3))
    blk_g = lambda WT: asc(WT.reshape(C, P, NEC, TH).transpose(2, 1, 0, 3))
    wq4, wk4, wv4 = blk_qkv(WqT), blk_qkv(WkT), blk_qkv(WvT)
    wo4, wf4, rr4 = blk_g(WoT), blk_g(WfT), blk_g(RrT)
    bcol = lambda b: asc(b.reshape(C, P).T)
    bq_f, bk_f, bv_f = bcol(bq), bcol(bk), bcol(bv)
    bo_r = asc(bo.reshape(1, D))

    nc = build_nc()

    in_maps = []
    for core in range(NCORES):
        b, j = core // 4, core % 4
        xt = asc(x[b, j * TLOC:(j + 1) * TLOC, :])
        xf = asc(xt.T)
        maskb = asc(np.broadcast_to(np.repeat(_mask_for(core), C)[None, :], (P, NCORES * C)))
        in_maps.append(dict(
            xt=xt, xf=xf, wq4=wq4, wk4=wk4, wv4=wv4, wo4=wo4, wf4=wf4, rr4=rr4,
            bq=bq_f, bk=bk_f, bv=bv_f, bo=bo_r, maskb=maskb))

    res = run_bass_kernel_spmd(nc, in_maps, core_ids=list(range(NCORES)))

    out = np.empty((BATCH, L, D), np.float32)
    for core in range(NCORES):
        b, j = core // 4, core % 4
        out[b, j * TLOC:(j + 1) * TLOC, :] = res.results[core]["out"]
    return out
